# version 01
# baseline (speedup 1.0000x reference)
import math
import contextlib
import numpy as np
import ml_dtypes
import concourse.bass as bass
import concourse.mybir as mybir
from concourse.bass_utils import run_bass_kernel_spmd

F32 = mybir.dt.float32
BF16 = mybir.dt.bfloat16
AF = mybir.ActivationFunctionType
ALU = mybir.AluOpType
AX = mybir.AxisListType

B, S, E = 4, 4096, 4096
H = 32
NCH = 32
QB = 512
NBLK = 4
SCALING = float(E // H) ** -0.5
_base = 2.0 ** (-(2.0 ** -2.0))
SLOPES = (_base ** np.arange(1, H + 1)).astype(np.float64)
WH = [min(512, 64 << (h // 4)) for h in range(H)]
KEEP = 90.0
LH = [min(S, int(math.ceil(KEEP / SLOPES[h]))) for h in range(H)]
BANDW = 1664
BOFF = 1024
EPS = 1e-5
SQ32 = 1.0 / math.sqrt(32.0)


def attn_slots(blk, h, tau):
    w = WH[h]
    qt0 = 1024 * blk + tau * w
    lo = qt0 - LH[h]
    hi = qt0 + 512 + w - 1
    out = []
    for j in range(max(0, lo // 128), min(NCH - 1, hi // 128) + 1):
        delta = 128 * j - qt0
        masked = (-127 <= delta < w) or (385 <= delta < 512 + w)
        out.append((j, delta, masked))
    return out


def build_bias_cols():
    cols = {}
    for blk in range(NBLK):
        for h in range(H):
            for tau in range(QB // WH[h]):
                for (_, delta, _) in attn_slots(blk, h, tau):
                    cols.setdefault((h, delta), len(cols))
    return cols


BIAS_COLS = build_bias_cols()
NBIAS = len(BIAS_COLS)


def host_bias_table(parity):
    t = np.zeros((128, NBIAS), np.float64)
    r = np.arange(128)
    for (h, delta), idx in BIAS_COLS.items():
        w = WH[h]
        dd = delta - 512 * parity
        if dd > w - 1:
            t[:, idx] = -1e30
        else:
            t[:, idx] = SLOPES[h] * (r + dd - (w - 1))
    return t.astype(np.float32)


def host_band(parity):
    r = np.arange(128)[:, None]
    y = np.arange(BANDW)[None, :]
    return (r <= y - BOFF + 512 * parity).astype(ml_dtypes.bfloat16)


ENG_SEM = {"vector": "dve", "scalar": "act"}
SEM_NAMES = ["pe", "act", "dve", "bigA", "bigB", "ws0", "ws1", "wqka", "init",
             "out0", "out1", "xq", "tap"]
ALL_ENGS = ("tensor", "scalar", "vector", "gpsimd", "sync")


class Plan:
    def __init__(self):
        self.steps = []
        self.cnt = {s: 0 for s in SEM_NAMES}

    def add(self, eng, fn, waits=None, inc=None, inc_amt=1):
        w = {k: v for k, v in (waits or {}).items() if v > 0}
        for s_, v in w.items():
            assert v <= self.cnt[s_], (s_, v, self.cnt[s_], eng)
        incs = []
        if eng in ENG_SEM:
            es = ENG_SEM[eng]
            w[es] = max(w.get(es, 0), self.cnt[es])
            incs.append((es, 1))
            self.cnt[es] += 1
        elif inc is not None:
            incs.append((inc, inc_amt))
            self.cnt[inc] += inc_amt
        self.steps.append((eng, w, fn, incs))
        return dict(self.cnt)

    def barrier(self, engines=ALL_ENGS):
        snap = {s: v for s, v in self.cnt.items() if v > 0}
        for e in engines:
            self.steps.append((e, dict(snap), None, []))

    def emit(self, eng_name, eng, sems):
        seen = {}
        for e_, w, fn, incs in self.steps:
            if e_ != eng_name:
                continue
            for s_, v in sorted(w.items()):
                if v > seen.get(s_, 0):
                    eng.wait_ge(sems[s_], v)
                    seen[s_] = v
            inst = fn(eng) if fn is not None else None
            for s_, amt in incs:
                inst.then_inc(sems[s_], amt)


def build_nc(nblk=NBLK, taps=()):
    nc = bass.Bass("TRN2", target_bir_lowering=False)
    P = Plan()
    C = P.cnt

    xT = nc.dram_tensor("xT", [E, S], F32, kind="ExternalInput").ap()
    xq_d = nc.dram_tensor("xq", [E, 2048], F32, kind="ExternalInput").ap()
    w_qkv = nc.dram_tensor("w_qkv", [E, 4352], F32, kind="ExternalInput").ap()
    w_c = nc.dram_tensor("w_c", [E, E], F32, kind="ExternalInput").ap()
    W_T = nc.dram_tensor("W_T", [E, E], F32, kind="ExternalInput").ap()
    Wqka = nc.dram_tensor("Wqka", [E, 640], F32, kind="ExternalInput").ap()
    BmT = nc.dram_tensor("BmT", [128, E], F32, kind="ExternalInput").ap()
    bvec = nc.dram_tensor("bvec", [1, E], F32, kind="ExternalInput").ap()
    btab = nc.dram_tensor("btab", [128, NBIAS], F32, kind="ExternalInput").ap()
    band = nc.dram_tensor("band", [128, BANDW], BF16, kind="ExternalInput").ap()
    ident = nc.dram_tensor("ident", [128, 128], BF16, kind="ExternalInput").ap()
    onesb = nc.dram_tensor("onesb", [128, 128], BF16, kind="ExternalInput").ap()
    cst = nc.dram_tensor("cst", [128, 4], F32, kind="ExternalInput").ap()
    x2T = nc.dram_tensor("x2T", [E, 2048], F32, kind="ExternalOutput").ap()
    TAPSHAPE = {"kv": [128, 8192], "q": [128, 16384], "attn": [128, 16384],
                "x1": [128, 16384], "ln2": [128, 16384]}
    tap_t = {n: nc.dram_tensor("tap_" + n, TAPSHAPE[n], F32, kind="ExternalOutput").ap()
             for n in taps}

    st = contextlib.ExitStack()
    with st:
        sbt = lambda n, s, d: st.enter_context(nc.sbuf_tensor(n, s, d))
        pst_ = lambda n, s, d: st.enter_context(nc.psum_tensor(n, s, d))
        band_sb = sbt("band_sb", [128, BANDW], BF16)
        btab_sb = sbt("btab_sb", [128, NBIAS], F32)
        id_sb = sbt("id_sb", [128, 128], BF16)
        ones_sb = sbt("ones_sb", [128, 128], BF16)
        bvec_sb = sbt("bvec_sb", [128, NCH], F32)
        cst_sb = sbt("cst_sb", [128, 4], F32)
        A1 = sbt("A1", [128, 16384], BF16)
        A2 = sbt("A2", [128, 16384], BF16)
        A3 = sbt("A3", [128, 20480], BF16)
        A4 = sbt("A4", [128, 16384], BF16)
        kT_sb = sbt("kT_sb", [128, S], BF16)
        vtok_sb = sbt("vtok_sb", [128, S], BF16)
        vstage = sbt("vstage", [128, 512], BF16)
        sq_sb = sbt("sq_sb", [128, 1024], BF16)
        ws_sb = sbt("ws_sb", [128, 8448], BF16)
        scr = sbt("scr", [128, 4096], BF16)
        rows_f = sbt("rows_f", [1, 1024], F32)
        rows_bf = sbt("rows_bf", [1, 1024], BF16)
        rl_bf = sbt("rl_bf", [1, 1024], BF16)
        mrep_sb = sbt("mrep_sb", [128, 512], BF16)
        rrep_sb = sbt("rrep_sb", [128, 512], BF16)
        fscr = sbt("fscr", [128, 512], F32)
        outst = sbt("outst", [128, 1024], F32)

        ps0 = pst_("ps0", [128, 512], F32)
        ps1 = pst_("ps1", [128, 512], F32)
        ps2 = pst_("ps2", [128, 512], F32)
        ps3 = pst_("ps3", [128, 512], F32)
        ps4 = pst_("ps4", [128, 512], F32)
        psLa = pst_("psLa", [1, 512], F32)
        psLb = pst_("psLb", [1, 512], F32)
        psT = pst_("psT", [128, 1024], BF16)

        sems = {n: st.enter_context(nc.semaphore(n)) for n in SEM_NAMES}
        block = st.enter_context(nc.Block())

        def dma(eng, out, in_, slot, waits=None, slow=False):
            P.add(eng, lambda e, o=out, i=in_: e.dma_start(
                out=o, in_=i, allow_slow_non_contiguous=slow) if slow else
                e.dma_start(out=o, in_=i), waits=waits, inc=slot, inc_amt=16)
            return C[slot]

        def mm(out_ap, lhsT, rhs, start, stop, waits=None, inc=None):
            P.add("tensor", lambda e, o=out_ap, l=lhsT, r=rhs, s1=start, s2=stop:
                  e.matmul(o, l, r, start=s1, stop=s2), waits=waits, inc=inc)
            return C["pe"]

        def tr(out_ap, in_ap, waits=None):
            P.add("tensor", lambda e, o=out_ap, i=in_ap: e.transpose(o, i, id_sb[:, :]),
                  waits=waits, inc="pe")
            return C["pe"]

        def act(fn, waits=None):
            P.add("scalar", fn, waits=waits)
            return C["act"]

        def dve(fn, waits=None):
            P.add("vector", fn, waits=waits)
            return C["dve"]

        dma("sync", band_sb[:, :], band, "init")
        dma("sync", btab_sb[:, :], btab, "init")
        dma("sync", id_sb[:, :], ident, "init")
        dma("sync", ones_sb[:, :], onesb, "init")
        dma("sync", cst_sb[:, :], cst, "init")
        dma("sync", bvec_sb[:, :],
            bvec.rearrange("o (c p) -> p o c", p=128).squeeze(1), "init", slow=True)
        dma("gpsimd", A4[:, 0:8192].rearrange("p (c n) -> p c n", c=NCH),
            w_qkv[:, 4096:4352].rearrange("(c p) n -> p c n", p=128), "init")
        INIT = C["init"]

        def stats_and_reps(buf, buf_ready, first_waits):
            entry_pe = C["pe"]
            sq_war = [entry_pe, entry_pe]
            last_sum = entry_pe
            for dc in range(NCH):
                sl = buf[:, dc * 512:(dc + 1) * 512]
                w = dict(buf_ready)
                w["pe"] = max(w.get("pe", 0), sq_war[dc % 2])
                if dc == 0:
                    for k_, v_ in first_waits.items():
                        w[k_] = max(w.get(k_, 0), v_)
                a = act(lambda e, s=sl, d=dc: e.activation(
                    sq_sb[:, (d % 2) * 512:(d % 2 + 1) * 512], s, AF.Square), waits=w)
                w2 = dict(buf_ready) if dc == 0 else None
                if dc == 0:
                    for k_, v_ in first_waits.items():
                        w2[k_] = max(w2.get(k_, 0), v_)
                mm(psLa[0:1, :], ones_sb[:, 0:1], sl, dc == 0, dc == NCH - 1, waits=w2)
                last_sum = mm(psLb[0:1, :], ones_sb[:, 0:1],
                              sq_sb[:, (dc % 2) * 512:(dc % 2 + 1) * 512],
                              dc == 0, dc == NCH - 1, waits={"act": a}, inc="pe")
                sq_war[dc % 2] = last_sum
            m_r = rows_f[0:1, 0:512]
            var = rows_f[0:1, 512:1024]
            dve(lambda e: e.tensor_scalar(m_r, psLa[0:1, :], 1.0 / 4096, None, ALU.mult),
                waits={"pe": last_sum})
            dve(lambda e: e.tensor_mul(var, m_r, m_r))
            dve(lambda e: e.scalar_tensor_tensor(var, psLb[0:1, :], 1.0 / 4096, var,
                                                 ALU.mult, ALU.subtract))
            a1 = act(lambda e: e.activation(var, var, AF.Sqrt, bias=cst_sb[0:1, 0:1]),
                     waits={"dve": C["dve"]})
            dve(lambda e: e.reciprocal(var, var), waits={"act": a1})
            dve(lambda e: e.tensor_copy(rows_bf[0:1, 0:512], m_r))
            dbf = dve(lambda e: e.tensor_copy(rows_bf[0:1, 512:1024], var))
            w = dict(first_waits)
            w["dve"] = dbf
            p0 = mm(ps0[:, :], ones_sb[0:1, :], rows_bf[0:1, 0:512], True, True,
                    waits=w, inc="pe")
            p1 = mm(ps1[:, :], ones_sb[0:1, :], rows_bf[0:1, 512:1024], True, True,
                    inc="pe")
            dve(lambda e: e.tensor_copy(mrep_sb[:, :], ps0[:, :]), waits={"pe": p0})
            drep = dve(lambda e: e.tensor_copy(rrep_sb[:, :], ps1[:, :]), waits={"pe": p1})
            return drep, C["pe"]

        ws_done = [{}, {}]

        def matmul_phase(weight_dram, rhs_buf, evac_fn, psum_free_init,
                         extra_mm=None, bmt_src=None):
            psum_free = [dict(psum_free_init), dict(psum_free_init)]
            slot_load = [0, 0]
            psq = [ps0, ps1]

            def load(oc):
                slot = oc % 2
                sem = "ws0" if slot == 0 else "ws1"
                slot_load[slot] = dma(
                    "gpsimd", ws_sb[:, slot * 4224:slot * 4224 + 4096]
                    .rearrange("p (c n) -> p c n", c=NCH),
                    weight_dram[:, oc * 128:(oc + 1) * 128]
                    .rearrange("(c p) n -> p c n", p=128),
                    sem, waits=ws_done[slot])
                if bmt_src is not None:
                    slot_load[slot] = dma(
                        "gpsimd", ws_sb[:, slot * 4224 + 4096:slot * 4224 + 4224],
                        bmt_src[:, oc * 128:(oc + 1) * 128], sem)

            load(0)
            for oc in range(NCH):
                if oc + 1 < NCH:
                    load(oc + 1)
                slot = oc % 2
                sem = "ws0" if slot == 0 else "ws1"
                pp = psq[slot]
                for dc in range(NCH):
                    w = None
                    if dc == 0:
                        w = dict(psum_free[slot])
                        w[sem] = slot_load[slot]
                    mm(pp[:, :],
                       ws_sb[:, slot * 4224 + dc * 128:slot * 4224 + (dc + 1) * 128],
                       rhs_buf[:, dc * 512:(dc + 1) * 512], dc == 0,
                       dc == NCH - 1 and extra_mm is None,
                       waits=w, inc="pe" if (dc == NCH - 1 and extra_mm is None) else None)
                if extra_mm is not None:
                    extra_mm(oc, pp, slot)
                pe_done = C["pe"]
                ws_done[slot] = {"pe": pe_done}
                psum_free[slot] = evac_fn(oc, pp, pe_done)
            return psum_free

        big = [A1, A2]
        big_sem = ["bigA", "bigB"]
        big_done = [{}, {}]
        psL_free = {}
        kv_free = [{}, {}]
        vT_war = {}
        ld_prev = dma("gpsimd", big[0].rearrange("p (c n) -> p c n", c=NCH),
                      xT[:, 0:512].rearrange("(c p) n -> p c n", p=128),
                      big_sem[0], waits={"init": INIT})
        for ci in range(8):
            buf = big[ci % 2]
            ld = ld_prev
            if ci + 1 < 8:
                ld_prev = dma("gpsimd", big[(ci + 1) % 2].rearrange("p (c n) -> p c n", c=NCH),
                              xT[:, (ci + 1) * 512:(ci + 2) * 512]
                              .rearrange("(c p) n -> p c n", p=128),
                              big_sem[(ci + 1) % 2], waits=big_done[(ci + 1) % 2])
            ready = {big_sem[ci % 2]: ld}
            fw = dict(psL_free)
            for k_, v_ in kv_free[0].items():
                fw[k_] = max(fw.get(k_, 0), v_)
            for k_, v_ in kv_free[1].items():
                fw[k_] = max(fw.get(k_, 0), v_)
            drep, pe_rep = stats_and_reps(buf, ready, fw)
            psL_free = {"dve": C["dve"]}
            lastd = drep
            for dc in range(NCH):
                sl = buf[:, dc * 512:(dc + 1) * 512]
                dve(lambda e, s=sl: e.tensor_sub(s, s, mrep_sb[:, :]),
                    waits={"pe": pe_rep})
                lastd = dve(lambda e, s=sl: e.tensor_mul(s, s, rrep_sb[:, :]))
            pv_ = 0
            for dc in range(NCH):
                w = {"dve": lastd} if dc == 0 else None
                mm(ps0[:, :], A4[:, dc * 256:dc * 256 + 128],
                   buf[:, dc * 512:(dc + 1) * 512], dc == 0, dc == NCH - 1, waits=w)
                pv_ = mm(ps1[:, :], A4[:, dc * 256 + 128:dc * 256 + 256],
                         buf[:, dc * 512:(dc + 1) * 512], dc == 0, dc == NCH - 1,
                         inc="pe" if dc == NCH - 1 else None)
            kd = act(lambda e, c=ci: e.activation(kT_sb[:, c * 512:(c + 1) * 512],
                                                  ps0[:, :], AF.Copy), waits={"pe": pv_})
            vd = act(lambda e: e.activation(vstage[:, :], ps1[:, :], AF.Copy))
            kv_free[0] = {"act": kd}
            kv_free[1] = {"act": vd}
            dvv = 0
            for t in range(4):
                w = {"act": vd}
                for k_, v_ in vT_war.items():
                    w[k_] = max(w.get(k_, 0), v_)
                pt = tr(psT[:, t * 128:(t + 1) * 128], vstage[:, t * 128:(t + 1) * 128],
                        waits=w)
                dvv = dve(lambda e, c=ci, t_=t: e.tensor_copy(
                    vtok_sb[:, (c * 4 + t_) * 128:(c * 4 + t_ + 1) * 128],
                    psT[:, t_ * 128:(t_ + 1) * 128]), waits={"pe": pt})
            vT_war = {"dve": dvv}
            big_done[ci % 2] = {"pe": pv_}

        P.barrier()
        if "kv" in taps:
            dma("gpsimd", tap_t["kv"][:, 0:4096], kT_sb[:, :], "tap")
            dma("gpsimd", tap_t["kv"][:, 4096:8192], vtok_sb[:, :], "tap")
            P.barrier()

        qT = A3
        attnT = A4
        ln_buf = A2
        EB = [0, 512, 1024]
        RL = [1536, 2048]
        wqka_ld = 0
        wqka_done = {}
        out_use = [0, 0]

        for blk in range(nblk):
            P.barrier()
            ldq = dma("gpsimd", A1.rearrange("p (c n) -> p c n", c=NCH),
                      xq_d[:, blk * 512:(blk + 1) * 512]
                      .rearrange("(c p) n -> p c n", p=128), "xq")
            drep, pe_rep = stats_and_reps(A1, {"xq": ldq}, {})
            for dc in range(NCH):
                s_in = A1[:, dc * 512:(dc + 1) * 512]
                s_out = ln_buf[:, dc * 512:(dc + 1) * 512]
                dve(lambda e, a_=s_out, b_=s_in: e.tensor_sub(a_, b_, mrep_sb[:, :]),
                    waits={"pe": pe_rep})
                dve(lambda e, a_=s_out: e.tensor_mul(a_, a_, rrep_sb[:, :]))
            ln1_done = C["dve"]

            def evac_q(oc, pp, pe_done):
                a = act(lambda e, o=oc, p_=pp: e.activation(
                    qT[:, o * 512:(o + 1) * 512], p_[:, :], AF.Copy),
                    waits={"pe": pe_done})
                return {"act": a}

            matmul_phase(w_qkv, ln_buf, evac_q, {"dve": ln1_done})
            if "q" in taps and blk == 0:
                P.barrier()
                dma("gpsimd", tap_t["q"], qT[:, 0:16384], "tap")
                P.barrier()

            P.barrier()
            sc_bank = [ps0, ps1]
            at_bank = [ps2, ps3]
            ls_bank = [psLa, psLb]
            sc_war = [{}, {}]
            at_war = [{}, {}]
            ls_war = [{}, {}]
            rl_war = [{}, {}]
            ps4_war = {}
            tg = 0
            cg = 0
            for h in range(H):
                w_h = WH[h]
                for tau in range(QB // w_h):
                    slots = attn_slots(blk, h, tau)
                    Ab = at_bank[tg % 2]
                    Lb = ls_bank[tg % 2]
                    qsl = qT[:, h * 512 + tau * w_h: h * 512 + (tau + 1) * w_h]
                    pv_cnt = {}
                    for idx, (j, delta, masked) in enumerate(slots):
                        Sb = sc_bank[cg % 2]
                        sc_pe = mm(Sb[:, 0:w_h], kT_sb[:, j * 128:(j + 1) * 128], qsl,
                                   True, True, waits=dict(sc_war[cg % 2]), inc="pe")
                        eb = EB[idx % 3]
                        wt = {"pe": sc_pe}
                        if idx >= 3:
                            wt["pe"] = max(wt["pe"], pv_cnt[idx - 3])
                        col = BIAS_COLS[(h, delta)]
                        a = act(lambda e, w_=w_h, eb_=eb, Sb_=Sb, c_=col: e.activation(
                            scr[:, eb_:eb_ + w_], Sb_[:, 0:w_], AF.Exp,
                            bias=btab_sb[:, c_:c_ + 1], scale=SCALING), waits=wt)
                        sc_war[cg % 2] = {"act": a}
                        pv_wait = {"act": a}
                        if masked:
                            off = BOFF - delta
                            d = dve(lambda e, w_=w_h, eb_=eb, o_=off: e.tensor_mul(
                                scr[:, eb_:eb_ + w_], scr[:, eb_:eb_ + w_],
                                band_sb[:, o_:o_ + w_]), waits={"act": a})
                            pv_wait = {"dve": d}
                        if idx == 0:
                            for src in (at_war[tg % 2], ls_war[tg % 2]):
                                for k_, v_ in src.items():
                                    pv_wait[k_] = max(pv_wait.get(k_, 0), v_)
                        mm(Ab[:, 0:w_h], vtok_sb[:, j * 128:(j + 1) * 128],
                           scr[:, eb:eb + w_h], idx == 0, idx == len(slots) - 1,
                           waits=pv_wait)
                        pv_cnt[idx] = mm(Lb[0:1, 0:w_h], ones_sb[:, 0:1],
                                         scr[:, eb:eb + w_h], idx == 0,
                                         idx == len(slots) - 1, inc="pe")
                        cg += 1
                    win_pe = pv_cnt[len(slots) - 1]
                    rlo = (tg % 2) * 512
                    wt = {"pe": win_pe}
                    for k_, v_ in rl_war[tg % 2].items():
                        wt[k_] = max(wt.get(k_, 0), v_)
                    dve(lambda e, w_=w_h, o=rlo, L_=Lb: e.reciprocal(
                        rows_f[0:1, o:o + w_], L_[0:1, 0:w_]), waits=wt)
                    drl = dve(lambda e, w_=w_h, o=rlo: e.tensor_copy(
                        rl_bf[0:1, o:o + w_], rows_f[0:1, o:o + w_]))
                    ls_war[tg % 2] = {"dve": drl}
                    p4 = mm(ps4[:, 0:w_h], ones_sb[0:1, :], rl_bf[0:1, rlo:rlo + w_h],
                            True, True,
                            waits={**ps4_war, "dve": max(drl, ps4_war.get("dve", 0))},
                            inc="pe")
                    rro = RL[tg % 2]
                    dve(lambda e, w_=w_h, o=rro: e.tensor_copy(
                        scr[:, o:o + w_], ps4[:, 0:w_]), waits={"pe": p4})
                    ps4_war = {"dve": C["dve"]}
                    dat = dve(lambda e, w_=w_h, o=rro, h_=h, t_=tau, A_=Ab: e.tensor_mul(
                        attnT[:, h_ * 512 + t_ * w_: h_ * 512 + t_ * w_ + w_],
                        A_[:, 0:w_], scr[:, o:o + w_]))
                    at_war[tg % 2] = {"dve": dat}
                    rl_war[tg % 2] = {"pe": p4}
                    tg += 1
            P.barrier()
            if "attn" in taps and blk == 0:
                dma("gpsimd", tap_t["attn"], attnT[:, :], "tap")
                P.barrier()

            def evac_wc(oc, pp, pe_done):
                d = dve(lambda e, o=oc, p_=pp: e.tensor_add(
                    A1[:, o * 512:(o + 1) * 512], p_[:, :],
                    A1[:, o * 512:(o + 1) * 512]), waits={"pe": pe_done})
                return {"dve": d}

            matmul_phase(w_c, attnT, evac_wc, {})
            wqka_ld = dma("gpsimd", A3[:, 0:20480].rearrange("p (c n) -> p c n", c=NCH),
                          Wqka.rearrange("(c p) n -> p c n", p=128), "wqka",
                          waits=wqka_done)
            P.barrier()
            if "x1" in taps and blk == 0:
                dma("gpsimd", tap_t["x1"], A1[:, :], "tap")
                P.barrier()

            drep, pe_rep = stats_and_reps(A1, {}, {})
            for dc in range(NCH):
                s_in = A1[:, dc * 512:(dc + 1) * 512]
                s_out = ln_buf[:, dc * 512:(dc + 1) * 512]
                dve(lambda e, a_=s_out, b_=s_in: e.tensor_sub(a_, b_, mrep_sb[:, :]),
                    waits={"pe": pe_rep})
                dve(lambda e, a_=s_out: e.tensor_mul(a_, a_, rrep_sb[:, :]))
            ln2_done = C["dve"]
            P.barrier()
            if "ln2" in taps and blk == 0:
                dma("gpsimd", tap_t["ln2"], ln_buf[:, :], "tap")
                P.barrier()

            wh_war = {}
            for tc in range(4):
                pe_h = 0
                for dc in range(NCH):
                    w = {"wqka": wqka_ld, "dve": ln2_done} if dc == 0 else None
                    lh = ln_buf[:, dc * 512 + tc * 128: dc * 512 + tc * 128 + 128]
                    mm(ps2[:, 0:512], lh, A3[:, dc * 640:dc * 640 + 512],
                       dc == 0, dc == NCH - 1, waits=w)
                    pe_h = mm(ps3[:, 0:128], lh, A3[:, dc * 640 + 512:(dc + 1) * 640],
                              dc == 0, dc == NCH - 1,
                              inc="pe" if dc == NCH - 1 else None)
                a1 = act(lambda e: e.activation(scr[:, 0:512], ps2[:, 0:512], AF.Copy),
                         waits={"pe": pe_h})
                a2 = act(lambda e: e.activation(scr[:, 512:640], ps3[:, 0:128], AF.Copy))
                dve(lambda e: e.tensor_mul(fscr[:, 0:256], scr[:, 0:256],
                                           scr[:, 256:512]), waits={"act": a1})
                dve(lambda e: e.reduce_sum(
                    fscr[:, 256:264],
                    fscr[:, 0:256].rearrange("p (a b_) -> p a b_", a=8), axis=AX.X))
                dve(lambda e: e.reduce_max(fscr[:, 264:265], fscr[:, 256:264],
                                           axis=AX.X, negate=True))
                dn = dve(lambda e: e.tensor_scalar(fscr[:, 265:266], fscr[:, 264:265],
                                                   SQ32, None, ALU.mult))
                ae = act(lambda e: e.activation(fscr[:, 272:280], fscr[:, 256:264],
                                                AF.Exp, bias=fscr[:, 265:266],
                                                scale=SQ32), waits={"dve": dn})
                dve(lambda e: e.reduce_sum(fscr[:, 280:281], fscr[:, 272:280],
                                           axis=AX.X), waits={"act": ae})
                dve(lambda e: e.reciprocal(fscr[:, 281:282], fscr[:, 280:281]))
                dve(lambda e: e.tensor_scalar(scr[:, 768:776], fscr[:, 272:280],
                                              fscr[:, 281:282], None, ALU.mult))
                wtb = scr[:, 768:776].unsqueeze(2).broadcast_to((128, 8, 16))
                dwh = dve(lambda e, wtb_=wtb: e.tensor_tensor(
                    scr[:, 640:768].rearrange("p (a b_) -> p a b_", a=8),
                    scr[:, 512:640].rearrange("p (a b_) -> p a b_", a=8),
                    wtb_, ALU.mult), waits={"act": a2})
                w = {"dve": dwh}
                for k_, v_ in wh_war.items():
                    w[k_] = max(w.get(k_, 0), v_)
                pt = tr(psT[:, 0:128], scr[:, 640:768], waits=w)
                dvw = dve(lambda e, t_=tc: e.tensor_copy(
                    scr[:, 1024 + t_ * 128:1024 + (t_ + 1) * 128], psT[:, 0:128]),
                    waits={"pe": pt})
                wh_war = {"dve": dvw}
            whT_ready = C["dve"]

            def extra_bm(oc, pp, slot):
                mm(pp[:, :], ws_sb[:, slot * 4224 + 4096:slot * 4224 + 4224],
                   scr[:, 1024:1536], False, True, waits={"dve": whT_ready}, inc="pe")

            def evac_ffn(oc, pp, pe_done):
                sl = oc % 2
                osem = "out0" if sl == 0 else "out1"
                d = dve(lambda e, o=oc, p_=pp, s_=sl: e.scalar_tensor_tensor(
                    outst[:, s_ * 512:(s_ + 1) * 512], p_[:, :], bvec_sb[:, o:o + 1],
                    A1[:, o * 512:(o + 1) * 512], ALU.add, ALU.add),
                    waits={"pe": pe_done, osem: out_use[sl]})
                out_use[sl] = dma(
                    "sync", x2T[oc * 128:(oc + 1) * 128, blk * 512:(blk + 1) * 512],
                    outst[:, sl * 512:(sl + 1) * 512], osem, waits={"dve": d})
                return {"dve": d}

            matmul_phase(W_T, ln_buf, evac_ffn, {"dve": ln2_done},
                         extra_mm=extra_bm, bmt_src=BmT)
            wqka_done = {"pe": C["pe"]}

        P.barrier()
        snap = {s: v for s, v in P.cnt.items() if v > 0}
        P.steps.append(("sync", dict(snap), None, []))
        P.steps.append(("gpsimd", dict(snap), None, []))

        @block.tensor
        def _(t):
            P.emit("tensor", t, sems)

        @block.scalar
        def _(s):
            P.emit("scalar", s, sems)

        @block.vector
        def _(v):
            P.emit("vector", v, sems)

        @block.gpsimd
        def _(g):
            P.emit("gpsimd", g, sems)

        @block.sync
        def _(s):
            P.emit("sync", s, sems)

    return nc


def make_core_inputs(inputs, core):
    b, parity = core // 2, core % 2
    x = np.asarray(inputs["L0_attn_hidden_states"])
    xT_b = np.ascontiguousarray(x[b].T)
    own = np.concatenate(
        [xT_b[:, 1024 * i + 512 * parity: 1024 * i + 512 * parity + 512]
         for i in range(4)], axis=1)
    Wq = np.asarray(inputs["L0_ffn_Wq"])
    Wk = np.asarray(inputs["L0_ffn_Wk"])
    A = np.asarray(inputs["L0_ffn_A"])
    Bm = np.asarray(inputs["L0_ffn_Bm"])
    return {
        "xT": xT_b,
        "xq": np.ascontiguousarray(own),
        "w_qkv": np.asarray(inputs["L0_attn_w_qkv"]),
        "w_c": np.asarray(inputs["L0_attn_w_c"]),
        "W_T": np.ascontiguousarray(np.asarray(inputs["L0_ffn_W"]).T),
        "Wqka": np.ascontiguousarray(np.concatenate(
            [Wq.T, Wk.T, A.reshape(128, 4096).T], axis=1)),
        "BmT": np.ascontiguousarray(Bm.transpose(0, 2, 1).reshape(128, 4096)),
        "bvec": np.ascontiguousarray(np.asarray(inputs["L0_ffn_b"]).reshape(1, 4096)),
        "btab": host_bias_table(parity),
        "band": host_band(parity),
        "cst": np.tile(np.array([[EPS, 1.0 / 4096, SQ32, 0.0]], np.float32), (128, 1)),
        "ident": np.eye(128, dtype=ml_dtypes.bfloat16),
        "onesb": np.ones((128, 128), dtype=ml_dtypes.bfloat16),
    }


def assemble_out(results):
    out = np.empty((B, S, E), np.float32)
    for core in range(8):
        b, parity = core // 2, core % 2
        o = np.asarray(results[core]["x2T"])
        for i in range(4):
            q0 = 1024 * i + 512 * parity
            out[b, q0:q0 + 512, :] = o[:, i * 512:(i + 1) * 512].T
    return out


def kernel(**inputs):
    nc = build_nc()
    in_maps = [make_core_inputs(inputs, c) for c in range(8)]
    res = run_bass_kernel_spmd(nc, in_maps, core_ids=list(range(8)))
    return assemble_out(res.results)


# version 03
# speedup vs baseline: 1.7265x; 1.7265x over previous
import math
import contextlib
import numpy as np
import ml_dtypes
import concourse.bass as bass
import concourse.mybir as mybir
from concourse.bass_utils import run_bass_kernel_spmd

F32 = mybir.dt.float32
BF16 = mybir.dt.bfloat16
AF = mybir.ActivationFunctionType
ALU = mybir.AluOpType
AX = mybir.AxisListType

B, S, E = 4, 4096, 4096
H = 32
NCH = 32
QB = 512
NBLK = 4
SCALING = float(E // H) ** -0.5
_base = 2.0 ** (-(2.0 ** -2.0))
SLOPES = (_base ** np.arange(1, H + 1)).astype(np.float64)
WH = [min(512, 64 << (h // 4)) for h in range(H)]
KEEP = 90.0
LH = [min(S, int(math.ceil(KEEP / SLOPES[h]))) for h in range(H)]
BANDW = 1664
BOFF = 1024
EPS = 1e-5
SQ32 = 1.0 / math.sqrt(32.0)


def attn_slots(blk, h, tau):
    w = WH[h]
    qt0 = 1024 * blk + tau * w
    lo = qt0 - LH[h]
    hi = qt0 + 512 + w - 1
    out = []
    for j in range(max(0, lo // 128), min(NCH - 1, hi // 128) + 1):
        delta = 128 * j - qt0
        masked = (-127 <= delta < w) or (385 <= delta < 512 + w)
        out.append((j, delta, masked))
    return out




def attn_chunks(blk, h):
    w = WH[h]
    per_j = {}
    for tau in range(QB // w):
        for (j, delta, masked) in attn_slots(blk, h, tau):
            per_j.setdefault(j, []).append((tau, delta, masked))
    out = []
    for j in sorted(per_j):
        taus = sorted(per_j[j])
        ta = taus[0][0]
        tb = taus[-1][0]
        assert [t[0] for t in taus] == list(range(ta, tb + 1))
        out.append((j, ta, tb, taus))
    return out


def build_bias_cols():
    cols = {}
    for blk in range(NBLK):
        for h in range(H):
            for tau in range(QB // WH[h]):
                for (_, delta, _) in attn_slots(blk, h, tau):
                    cols.setdefault((h, delta), len(cols))
    return cols


BIAS_COLS = build_bias_cols()
NBIAS = len(BIAS_COLS)


def host_bias_table(parity):
    t = np.zeros((128, NBIAS), np.float64)
    r = np.arange(128)
    for (h, delta), idx in BIAS_COLS.items():
        w = WH[h]
        dd = delta - 512 * parity
        if dd > w - 1:
            t[:, idx] = -1e30
        else:
            t[:, idx] = SLOPES[h] * (r + dd - (w - 1))
    return t.astype(np.float32)


def host_band(parity):
    r = np.arange(128)[:, None]
    y = np.arange(BANDW)[None, :]
    return (r <= y - BOFF + 512 * parity).astype(ml_dtypes.bfloat16)


ENG_SEM = {"vector": "dve", "scalar": "act"}
SEM_NAMES = ["pe", "act", "dve", "gp", "bigA", "bigB", "ws0", "ws1", "wqka", "init",
             "wkv", "out0", "out1", "xq", "tap"]
ALL_ENGS = ("tensor", "scalar", "vector", "gpsimd", "sync")


class Plan:
    def __init__(self):
        self.steps = []
        self.cnt = {s: 0 for s in SEM_NAMES}

    def add(self, eng, fn, waits=None, inc=None, inc_amt=1):
        w = {k: v for k, v in (waits or {}).items() if v > 0}
        for s_, v in w.items():
            assert v <= self.cnt[s_], (s_, v, self.cnt[s_], eng)
        incs = []
        if eng in ENG_SEM:
            es = ENG_SEM[eng]
            w[es] = max(w.get(es, 0), self.cnt[es])
            incs.append((es, 1))
            self.cnt[es] += 1
        elif inc is not None:
            incs.append((inc, inc_amt))
            self.cnt[inc] += inc_amt
        self.steps.append((eng, w, fn, incs))
        return dict(self.cnt)

    def barrier(self, engines=ALL_ENGS):
        snap = {s: v for s, v in self.cnt.items() if v > 0}
        for e in engines:
            self.steps.append((e, dict(snap), None, []))

    def emit(self, eng_name, eng, sems):
        seen = {}
        for e_, w, fn, incs in self.steps:
            if e_ != eng_name:
                continue
            for s_, v in sorted(w.items()):
                if v > seen.get(s_, 0):
                    eng.wait_ge(sems[s_], v)
                    seen[s_] = v
            inst = fn(eng) if fn is not None else None
            for s_, amt in incs:
                inst.then_inc(sems[s_], amt)


def build_nc(nblk=NBLK, taps=()):
    nc = bass.Bass("TRN2", target_bir_lowering=False)
    P = Plan()
    C = P.cnt

    xT_t = nc.dram_tensor("xT_t", [8, 128, 16384], F32, kind="ExternalInput").ap()
    xq_t = nc.dram_tensor("xq_t", [4, 128, 16384], F32, kind="ExternalInput").ap()
    wq_t = nc.dram_tensor("wq_t", [NCH, 128, 4096], F32, kind="ExternalInput").ap()
    wc_t = nc.dram_tensor("wc_t", [NCH, 128, 4096], F32, kind="ExternalInput").ap()
    wffn_t = nc.dram_tensor("wffn_t", [NCH, 128, 4224], F32, kind="ExternalInput").ap()
    wqka_t = nc.dram_tensor("wqka_t", [128, 20480], F32, kind="ExternalInput").ap()
    wkv_t = nc.dram_tensor("wkv_t", [128, 8192], F32, kind="ExternalInput").ap()
    bvec = nc.dram_tensor("bvec", [1, E], F32, kind="ExternalInput").ap()
    btab = nc.dram_tensor("btab", [128, NBIAS], F32, kind="ExternalInput").ap()
    band = nc.dram_tensor("band", [128, BANDW], BF16, kind="ExternalInput").ap()
    ident = nc.dram_tensor("ident", [128, 128], BF16, kind="ExternalInput").ap()
    onesb = nc.dram_tensor("onesb", [128, 128], BF16, kind="ExternalInput").ap()
    cst = nc.dram_tensor("cst", [128, 4], F32, kind="ExternalInput").ap()
    x2T = nc.dram_tensor("x2T", [E, 2048], F32, kind="ExternalOutput").ap()
    TAPSHAPE = {"kv": [128, 8192], "q": [128, 16384], "attn": [128, 16384],
                "x1": [128, 16384], "ln2": [128, 16384]}
    tap_t = {n: nc.dram_tensor("tap_" + n, TAPSHAPE[n], F32, kind="ExternalOutput").ap()
             for n in taps}

    st = contextlib.ExitStack()
    with st:
        sbt = lambda n, s, d: st.enter_context(nc.sbuf_tensor(n, s, d))
        pst_ = lambda n, s, d: st.enter_context(nc.psum_tensor(n, s, d))
        band_sb = sbt("band_sb", [128, BANDW], BF16)
        btab_sb = sbt("btab_sb", [128, NBIAS], F32)
        id_sb = sbt("id_sb", [128, 128], BF16)
        ones_sb = sbt("ones_sb", [128, 128], BF16)
        bvec_sb = sbt("bvec_sb", [128, NCH], F32)
        cst_sb = sbt("cst_sb", [128, 4], F32)
        A1 = sbt("A1", [128, 16384], BF16)
        A2 = sbt("A2", [128, 16384], BF16)
        A3 = sbt("A3", [128, 20480], BF16)
        A4 = sbt("A4", [128, 16384], BF16)
        kT_sb = sbt("kT_sb", [128, S], BF16)
        vtok_sb = sbt("vtok_sb", [128, S], BF16)
        vstage = sbt("vstage", [128, 512], BF16)
        sq_sb = sbt("sq_sb", [128, 1024], BF16)
        ws_sb = sbt("ws_sb", [128, 8448], BF16)
        scr = sbt("scr", [128, 4096], BF16)
        rows_f = sbt("rows_f", [1, 1024], F32)
        rows_bf = sbt("rows_bf", [1, 1024], BF16)
        rl_bf = sbt("rl_bf", [1, 1024], BF16)
        mrep_sb = sbt("mrep_sb", [128, 512], BF16)
        rrep_sb = sbt("rrep_sb", [128, 512], BF16)
        fscr = sbt("fscr", [128, 512], F32)
        outst = sbt("outst", [128, 1024], F32)

        ps0 = pst_("ps0", [128, 512], F32)
        ps1 = pst_("ps1", [128, 512], F32)
        ps2 = pst_("ps2", [128, 512], F32)
        ps3 = pst_("ps3", [128, 512], F32)
        ps4 = pst_("ps4", [128, 512], F32)
        psLa = pst_("psLa", [1, 512], F32)
        psLb = pst_("psLb", [1, 512], F32)
        psT_f = pst_("psT", [128, 512], F32)
        psT = psT_f[:, :].bitcast(BF16)

        sems = {n: st.enter_context(nc.semaphore(n)) for n in SEM_NAMES}
        block = st.enter_context(nc.Block())

        def dma(eng, out, in_, slot, waits=None, slow=False):
            P.add(eng, lambda e, o=out, i=in_: e.dma_start(
                out=o, in_=i, allow_slow_non_contiguous=slow) if slow else
                e.dma_start(out=o, in_=i), waits=waits, inc=slot, inc_amt=16)
            return C[slot]

        def mm(out_ap, lhsT, rhs, start, stop, waits=None, inc=None):
            P.add("tensor", lambda e, o=out_ap, l=lhsT, r=rhs, s1=start, s2=stop:
                  e.matmul(o, l, r, start=s1, stop=s2), waits=waits, inc=inc)
            return C["pe"]

        def tr(out_ap, in_ap, waits=None):
            P.add("tensor", lambda e, o=out_ap, i=in_ap: e.transpose(o, i, id_sb[:, :]),
                  waits=waits, inc="pe")
            return C["pe"]

        def act(fn, waits=None):
            P.add("scalar", fn, waits=waits)
            return C["act"]

        def dve(fn, waits=None):
            P.add("vector", fn, waits=waits)
            return C["dve"]

        dma("sync", band_sb[:, :], band, "init")
        dma("sync", btab_sb[:, :], btab, "init")
        dma("sync", id_sb[:, :], ident, "init")
        dma("sync", ones_sb[:, :], onesb, "init")
        dma("sync", cst_sb[:, :], cst, "init")
        dma("sync", bvec_sb[:, :],
            bvec.rearrange("o (c p) -> p o c", p=128).squeeze(1), "init", slow=True)
        dma("gpsimd", A4[:, 0:8192], wkv_t, "wkv")
        INIT = C["init"]
        WKV = C["wkv"]

        def stats_sums(buf, buf_ready, first_waits):
            entry_pe = C["pe"]
            sq_war = [entry_pe, entry_pe]
            last_sum = entry_pe
            for dc in range(NCH):
                sl = buf[:, dc * 512:(dc + 1) * 512]
                w = dict(buf_ready)
                w["pe"] = max(w.get("pe", 0), sq_war[dc % 2])
                if dc == 0:
                    for k_, v_ in first_waits.items():
                        w[k_] = max(w.get(k_, 0), v_)
                a = act(lambda e, s=sl, d=dc: e.activation(
                    sq_sb[:, (d % 2) * 512:(d % 2 + 1) * 512], s, AF.Square), waits=w)
                w2 = None
                if dc == 0:
                    w2 = dict(buf_ready)
                    for k_, v_ in first_waits.items():
                        w2[k_] = max(w2.get(k_, 0), v_)
                mm(psLa[0:1, :], ones_sb[:, 0:1], sl, dc == 0, dc == NCH - 1, waits=w2)
                last_sum = mm(psLb[0:1, :], ones_sb[:, 0:1],
                              sq_sb[:, (dc % 2) * 512:(dc % 2 + 1) * 512],
                              dc == 0, dc == NCH - 1, waits={"act": a}, inc="pe")
                sq_war[dc % 2] = last_sum
            return last_sum

        def stats_rows(last_sum):
            m_r = rows_f[0:1, 0:512]
            var = rows_f[0:1, 512:1024]
            dve(lambda e: e.tensor_scalar(m_r, psLa[0:1, :], 1.0 / 4096, None, ALU.mult),
                waits={"pe": last_sum})
            dve(lambda e: e.tensor_mul(var, m_r, m_r))
            dve(lambda e: e.scalar_tensor_tensor(var, psLb[0:1, :], 1.0 / 4096, var,
                                                 ALU.mult, ALU.subtract))
            a1 = act(lambda e: e.activation(var, var, AF.Sqrt, bias=cst_sb[0:1, 0:1]),
                     waits={"dve": C["dve"]})
            return dve(lambda e: e.reciprocal(var, var), waits={"act": a1})

        def stats_tail(last_sum, first_waits):
            stats_rows(last_sum)
            dve(lambda e: e.tensor_copy(rows_bf[0:1, 0:512], rows_f[0:1, 0:512]))
            dbf = dve(lambda e: e.tensor_copy(rows_bf[0:1, 512:1024],
                                              rows_f[0:1, 512:1024]))
            w = dict(first_waits)
            w["dve"] = dbf
            p0 = mm(ps0[:, :], ones_sb[0:1, :], rows_bf[0:1, 0:512], True, True,
                    waits=w, inc="pe")
            p1 = mm(ps1[:, :], ones_sb[0:1, :], rows_bf[0:1, 512:1024], True, True,
                    inc="pe")
            dve(lambda e: e.tensor_copy(mrep_sb[:, :], ps0[:, :]), waits={"pe": p0})
            drep = dve(lambda e: e.tensor_copy(rrep_sb[:, :], ps1[:, :]), waits={"pe": p1})
            return drep, C["pe"]

        def stats_and_reps(buf, buf_ready, first_waits):
            last_sum = stats_sums(buf, buf_ready, first_waits)
            return stats_tail(last_sum, first_waits)

        ws_done = [{}, {}]

        def matmul_phase(weight_dram, rhs_buf, evac_fn, psum_free_init,
                         extra_mm=None, bmt_src=None, post_evac=None):
            psum_free = [dict(psum_free_init), dict(psum_free_init)]
            slot_load = [0, 0]
            psq = [ps0, ps1]

            wid = 4224 if bmt_src else 4096

            def load(oc):
                slot = oc % 2
                sem = "ws0" if slot == 0 else "ws1"
                slot_load[slot] = dma(
                    "gpsimd", ws_sb[:, slot * 4224:slot * 4224 + wid],
                    weight_dram[oc], sem, waits=ws_done[slot])

            load(0)
            for oc in range(NCH):
                if oc + 1 < NCH:
                    load(oc + 1)
                slot = oc % 2
                sem = "ws0" if slot == 0 else "ws1"
                pp = psq[slot]
                for dc in range(NCH):
                    w = None
                    if dc == 0:
                        w = dict(psum_free[slot])
                        w[sem] = slot_load[slot]
                    mm(pp[:, :],
                       ws_sb[:, slot * 4224 + dc * 128:slot * 4224 + (dc + 1) * 128],
                       rhs_buf[:, dc * 512:(dc + 1) * 512], dc == 0,
                       dc == NCH - 1 and extra_mm is None,
                       waits=w, inc="pe" if (dc == NCH - 1 and extra_mm is None) else None)
                if extra_mm is not None:
                    extra_mm(oc, pp, slot)
                pe_done = C["pe"]
                ws_done[slot] = {"pe": pe_done}
                psum_free[slot] = evac_fn(oc, pp, pe_done)
                if post_evac is not None:
                    post_evac(oc, psum_free[slot])
            return psum_free

        big = [A1, A2]
        big_sem = ["bigA", "bigB"]
        big_done = [{}, {}]
        psL_free = {}
        kv_free = [{}, {}]
        vT_war = {}
        ld_prev = dma("gpsimd", big[0][:, :], xT_t[0], big_sem[0], waits={"init": INIT})
        for ci in range(8):
            buf = big[ci % 2]
            ld = ld_prev
            if ci + 1 < 8:
                ld_prev = dma("gpsimd", big[(ci + 1) % 2][:, :], xT_t[ci + 1],
                              big_sem[(ci + 1) % 2], waits=big_done[(ci + 1) % 2])
            ready = {big_sem[ci % 2]: ld}
            fw = dict(psL_free)
            for k_, v_ in kv_free[0].items():
                fw[k_] = max(fw.get(k_, 0), v_)
            for k_, v_ in kv_free[1].items():
                fw[k_] = max(fw.get(k_, 0), v_)
            drep, pe_rep = stats_and_reps(buf, ready, fw)
            psL_free = {"dve": C["dve"]}
            lastd = drep
            lastg = 0
            for dc in range(NCH):
                sl = buf[:, dc * 512:(dc + 1) * 512]
                if dc % 2 == 0:
                    lastd = dve(lambda e, s=sl: e.tensor_sub(s, s, mrep_sb[:, :]),
                                waits={"pe": pe_rep})
                else:
                    P.add("gpsimd", lambda e, s=sl: e.tensor_sub(s, s, mrep_sb[:, :]),
                          waits={"pe": pe_rep, "dve": drep, "gp": lastg}, inc="gp")
                    lastg = C["gp"]
            pv_ = 0
            for dc in range(NCH):
                w = None
                if dc == 0:
                    w = dict(ready)
                    w.update({"dve": lastd, "gp": lastg, "wkv": WKV})
                mm(ps0[:, :], A4[:, dc * 256:dc * 256 + 128],
                   buf[:, dc * 512:(dc + 1) * 512], dc == 0, dc == NCH - 1, waits=w)
                pv_ = mm(ps1[:, :], A4[:, dc * 256 + 128:dc * 256 + 256],
                         buf[:, dc * 512:(dc + 1) * 512], dc == 0, dc == NCH - 1,
                         inc="pe" if dc == NCH - 1 else None)
            dve(lambda e, c=ci: e.tensor_mul(kT_sb[:, c * 512:(c + 1) * 512],
                                             ps0[:, :], rrep_sb[:, :]),
                waits={"pe": pv_})
            vev = dve(lambda e: e.tensor_mul(vstage[:, :], ps1[:, :], rrep_sb[:, :]))
            kv_free[0] = {"dve": vev}
            kv_free[1] = {"dve": vev}
            dvv = 0
            for t in range(4):
                w = {"dve": vev}
                for k_, v_ in vT_war.items():
                    w[k_] = max(w.get(k_, 0), v_)
                pt = tr(psT[:, t * 128:(t + 1) * 128], vstage[:, t * 128:(t + 1) * 128],
                        waits=w)
                dvv = dve(lambda e, c=ci, t_=t: e.tensor_copy(
                    vtok_sb[:, (c * 4 + t_) * 128:(c * 4 + t_ + 1) * 128],
                    psT[:, t_ * 128:(t_ + 1) * 128]), waits={"pe": pt})
            vT_war = {"dve": dvv}
            big_done[ci % 2] = {"pe": pv_}

        P.barrier()
        if "kv" in taps:
            dma("gpsimd", tap_t["kv"][:, 0:4096], kT_sb[:, :], "tap")
            dma("gpsimd", tap_t["kv"][:, 4096:8192], vtok_sb[:, :], "tap")
            P.barrier()

        qT = A3
        attnT = A4
        ln_buf = A2
        EB = [0, 512, 1024]
        RL = [1536, 2048]
        wqka_ld = 0
        wqka_done = {}
        out_use = [0, 0]

        for blk in range(nblk):
            P.barrier()
            ldq = dma("gpsimd", A1[:, :], xq_t[blk], "xq")
            drep, pe_rep = stats_and_reps(A1, {"xq": ldq}, {})
            for dc in range(NCH):
                s_in = A1[:, dc * 512:(dc + 1) * 512]
                s_out = ln_buf[:, dc * 512:(dc + 1) * 512]
                dve(lambda e, a_=s_out, b_=s_in: e.tensor_sub(a_, b_, mrep_sb[:, :]),
                    waits={"pe": pe_rep})
                dve(lambda e, a_=s_out: e.tensor_mul(a_, a_, rrep_sb[:, :]))
            ln1_done = C["dve"]

            def evac_q(oc, pp, pe_done):
                a = act(lambda e, o=oc, p_=pp: e.activation(
                    qT[:, o * 512:(o + 1) * 512], p_[:, :], AF.Copy),
                    waits={"pe": pe_done})
                return {"act": a}

            matmul_phase(wq_t, ln_buf, evac_q, {"dve": ln1_done})
            if "q" in taps and blk == 0:
                P.barrier()
                dma("gpsimd", tap_t["q"], qT[:, 0:16384], "tap")
                P.barrier()

            P.barrier()
            sc_bank = [ps0, ps1, psT_f]
            at_bank = [ps2, ps3]
            ls_bank = [psLa, psLb]
            sc_war = [{}, {}, {}]
            at_war = [{}, {}]
            ls_war = [{}, {}]
            rlbf_war = [{}, {}]
            ps4_war = {}
            cg = 0
            for h in range(H):
                w_h = WH[h]
                chunks = attn_chunks(blk, h)
                nch = len(chunks)
                Ab = at_bank[h % 2]
                Lb = ls_bank[h % 2]
                first = [True]
                pend = []
                chunk_pv = {}
                LAG = 2

                def do_pv(k):
                    idx2, j2, ta2, taus2, pv_wait = pend[k]
                    last = idx2 == nch - 1
                    for ti, (tau, delta, masked) in enumerate(taus2):
                        w = pv_wait if ti == 0 else None
                        if first[0]:
                            w = dict(pv_wait)
                            for srcw in (at_war[h % 2], ls_war[h % 2]):
                                for k_, v_ in srcw.items():
                                    w[k_] = max(w.get(k_, 0), v_)
                        lastmm = last and ti == len(taus2) - 1
                        mm(Ab[:, tau * w_h:(tau + 1) * w_h],
                           vtok_sb[:, j2 * 128:(j2 + 1) * 128],
                           scr[:, EB[idx2 % 3] + (tau - ta2) * w_h:
                               EB[idx2 % 3] + (tau - ta2 + 1) * w_h],
                           first[0], lastmm, waits=w)
                        mm(Lb[0:1, tau * w_h:(tau + 1) * w_h], ones_sb[:, 0:1],
                           scr[:, EB[idx2 % 3] + (tau - ta2) * w_h:
                               EB[idx2 % 3] + (tau - ta2 + 1) * w_h],
                           first[0], lastmm,
                           inc="pe" if ti == len(taus2) - 1 else None)
                        first[0] = False
                    chunk_pv[idx2] = C["pe"]

                for idx, (j, ta, tb, taus) in enumerate(chunks):
                    Sb = sc_bank[cg % 3]
                    span = (tb - ta + 1) * w_h
                    sc_pe = mm(Sb[:, 0:span], kT_sb[:, j * 128:(j + 1) * 128],
                               qT[:, h * 512 + ta * w_h: h * 512 + (tb + 1) * w_h],
                               True, True, waits=dict(sc_war[cg % 3]), inc="pe")
                    eb = EB[idx % 3]
                    a = 0
                    gpw = 0
                    for ti, (tau, delta, masked) in enumerate(taus):
                        wt = {"pe": sc_pe}
                        if idx >= 3 and ti == 0:
                            wt["pe"] = max(wt["pe"], chunk_pv.get(idx - 3, 0))
                        col = BIAS_COLS[(h, delta)]
                        a = act(lambda e, w_=w_h, o1=eb + ti * w_h, o2=ti * w_h,
                                Sb_=Sb, c_=col: e.activation(
                            scr[:, o1:o1 + w_], Sb_[:, o2:o2 + w_], AF.Exp,
                            bias=btab_sb[:, c_:c_ + 1], scale=SCALING), waits=wt)
                        if masked:
                            off = BOFF - delta
                            P.add("gpsimd", lambda e, w_=w_h, o1=eb + ti * w_h,
                                  o_=off: e.tensor_mul(
                                      scr[:, o1:o1 + w_], scr[:, o1:o1 + w_],
                                      band_sb[:, o_:o_ + w_]),
                                  waits={"act": a}, inc="gp")
                            gpw = C["gp"]
                    sc_war[cg % 3] = {"act": a}
                    pv_wait = {"act": a}
                    if gpw:
                        pv_wait["gp"] = gpw
                    pend.append((idx, j, ta, taus, pv_wait))
                    if len(pend) > LAG:
                        do_pv(len(pend) - 1 - LAG)
                    cg += 1
                for k in range(max(0, len(pend) - LAG), len(pend)):
                    do_pv(k)
                win_pe = chunk_pv[nch - 1]
                rlo = (h % 2) * 512
                wt = {"pe": win_pe}
                for k_, v_ in rlbf_war[h % 2].items():
                    wt[k_] = max(wt.get(k_, 0), v_)
                lcast = act(lambda e, o=rlo, L_=Lb: e.activation(
                    rl_bf[0:1, o:o + 512], L_[0:1, 0:512], AF.Copy), waits=wt)
                ls_war[h % 2] = {"act": lcast}
                p4 = mm(ps4[:, 0:512], ones_sb[0:1, :], rl_bf[0:1, rlo:rlo + 512],
                        True, True,
                        waits={**ps4_war, "act": max(lcast, ps4_war.get("act", 0))},
                        inc="pe")
                rlbf_war[h % 2] = {"pe": p4}
                dve(lambda e: e.reciprocal(fscr[:, 0:512], ps4[:, 0:512]),
                    waits={"pe": p4})
                ps4_war = {"dve": C["dve"]}
                dat = dve(lambda e, h_=h, A_=Ab: e.tensor_mul(
                    attnT[:, h_ * 512:(h_ + 1) * 512], A_[:, 0:512], fscr[:, 0:512]))
                at_war[h % 2] = {"dve": dat}
            P.barrier()
            if "attn" in taps and blk == 0:
                dma("gpsimd", tap_t["attn"], attnT[:, :], "tap")
                P.barrier()

            def evac_wc(oc, pp, pe_done):
                d = dve(lambda e, o=oc, p_=pp: e.tensor_add(
                    A1[:, o * 512:(o + 1) * 512], p_[:, :],
                    A1[:, o * 512:(o + 1) * 512]), waits={"pe": pe_done})
                return {"dve": d}

            s2 = {"sq_war": [C["pe"], C["pe"]], "last": C["pe"]}

            def post_wc(oc, pfree):
                sl = A1[:, oc * 512:(oc + 1) * 512]
                w = dict(pfree)
                w["pe"] = max(w.get("pe", 0), s2["sq_war"][oc % 2])
                a = act(lambda e, s=sl, d=oc: e.activation(
                    sq_sb[:, (d % 2) * 512:(d % 2 + 1) * 512], s, AF.Square), waits=w)
                mm(psLa[0:1, :], ones_sb[:, 0:1], sl, oc == 0, oc == NCH - 1,
                   waits=dict(pfree))
                s2["last"] = mm(psLb[0:1, :], ones_sb[:, 0:1],
                                sq_sb[:, (oc % 2) * 512:(oc % 2 + 1) * 512],
                                oc == 0, oc == NCH - 1, waits={"act": a}, inc="pe")
                s2["sq_war"][oc % 2] = s2["last"]

            matmul_phase(wc_t, attnT, evac_wc, {}, post_evac=post_wc)
            wqka_ld = dma("gpsimd", A3[:, 0:20480], wqka_t, "wqka", waits=wqka_done)
            P.barrier()
            if "x1" in taps and blk == 0:
                dma("gpsimd", tap_t["x1"], A1[:, :], "tap")
                P.barrier()

            drep, pe_rep = stats_tail(s2["last"], {})
            for dc in range(NCH):
                s_in = A1[:, dc * 512:(dc + 1) * 512]
                s_out = ln_buf[:, dc * 512:(dc + 1) * 512]
                dve(lambda e, a_=s_out, b_=s_in: e.tensor_sub(a_, b_, mrep_sb[:, :]),
                    waits={"pe": pe_rep})
                dve(lambda e, a_=s_out: e.tensor_mul(a_, a_, rrep_sb[:, :]))
            ln2_done = C["dve"]
            P.barrier()
            if "ln2" in taps and blk == 0:
                dma("gpsimd", tap_t["ln2"], ln_buf[:, :], "tap")
                P.barrier()

            wh_war = {}
            for tc in range(4):
                pe_h = 0
                for dc in range(NCH):
                    w = {"wqka": wqka_ld, "dve": ln2_done} if dc == 0 else None
                    lh = ln_buf[:, dc * 512 + tc * 128: dc * 512 + tc * 128 + 128]
                    mm(ps2[:, 0:512], lh, A3[:, dc * 640:dc * 640 + 512],
                       dc == 0, dc == NCH - 1, waits=w)
                    pe_h = mm(ps3[:, 0:128], lh, A3[:, dc * 640 + 512:(dc + 1) * 640],
                              dc == 0, dc == NCH - 1,
                              inc="pe" if dc == NCH - 1 else None)
                a1 = act(lambda e: e.activation(scr[:, 0:512], ps2[:, 0:512], AF.Copy),
                         waits={"pe": pe_h})
                a2 = act(lambda e: e.activation(scr[:, 512:640], ps3[:, 0:128], AF.Copy))
                dve(lambda e: e.tensor_mul(fscr[:, 0:256], scr[:, 0:256],
                                           scr[:, 256:512]), waits={"act": a1})
                dve(lambda e: e.reduce_sum(
                    fscr[:, 256:264],
                    fscr[:, 0:256].rearrange("p (a b_) -> p a b_", a=8), axis=AX.X))
                dve(lambda e: e.reduce_max(fscr[:, 264:265], fscr[:, 256:264],
                                           axis=AX.X, negate=True))
                dn = dve(lambda e: e.tensor_scalar(fscr[:, 265:266], fscr[:, 264:265],
                                                   SQ32, None, ALU.mult))
                ae = act(lambda e: e.activation(fscr[:, 272:280], fscr[:, 256:264],
                                                AF.Exp, bias=fscr[:, 265:266],
                                                scale=SQ32), waits={"dve": dn})
                dve(lambda e: e.reduce_sum(fscr[:, 280:281], fscr[:, 272:280],
                                           axis=AX.X), waits={"act": ae})
                dve(lambda e: e.reciprocal(fscr[:, 281:282], fscr[:, 280:281]))
                dve(lambda e: e.tensor_scalar(scr[:, 768:776], fscr[:, 272:280],
                                              fscr[:, 281:282], None, ALU.mult))
                wtb = scr[:, 768:776].unsqueeze(2).broadcast_to((128, 8, 16))
                dwh = dve(lambda e, wtb_=wtb: e.tensor_tensor(
                    scr[:, 640:768].rearrange("p (a b_) -> p a b_", a=8),
                    scr[:, 512:640].rearrange("p (a b_) -> p a b_", a=8),
                    wtb_, ALU.mult), waits={"act": a2})
                w = {"dve": dwh}
                for k_, v_ in wh_war.items():
                    w[k_] = max(w.get(k_, 0), v_)
                pt = tr(psT[:, 0:128], scr[:, 640:768], waits=w)
                dvw = dve(lambda e, t_=tc: e.tensor_copy(
                    scr[:, 1024 + t_ * 128:1024 + (t_ + 1) * 128], psT[:, 0:128]),
                    waits={"pe": pt})
                wh_war = {"dve": dvw}
            whT_ready = C["dve"]

            def extra_bm(oc, pp, slot):
                mm(pp[:, :], ws_sb[:, slot * 4224 + 4096:slot * 4224 + 4224],
                   scr[:, 1024:1536], False, True, waits={"dve": whT_ready}, inc="pe")

            def evac_ffn(oc, pp, pe_done):
                sl = oc % 2
                osem = "out0" if sl == 0 else "out1"
                d = dve(lambda e, o=oc, p_=pp, s_=sl: e.scalar_tensor_tensor(
                    outst[:, s_ * 512:(s_ + 1) * 512], p_[:, :], bvec_sb[:, o:o + 1],
                    A1[:, o * 512:(o + 1) * 512], ALU.add, ALU.add),
                    waits={"pe": pe_done, osem: out_use[sl]})
                out_use[sl] = dma(
                    "sync", x2T[oc * 128:(oc + 1) * 128, blk * 512:(blk + 1) * 512],
                    outst[:, sl * 512:(sl + 1) * 512], osem, waits={"dve": d})
                return {"dve": d}

            matmul_phase(wffn_t, ln_buf, evac_ffn, {"dve": ln2_done},
                         extra_mm=extra_bm, bmt_src=True)
            wqka_done = {"pe": C["pe"]}

        P.barrier()
        snap = {s: v for s, v in P.cnt.items() if v > 0}
        P.steps.append(("sync", dict(snap), None, []))
        P.steps.append(("gpsimd", dict(snap), None, []))

        @block.tensor
        def _(t):
            P.emit("tensor", t, sems)

        @block.scalar
        def _(s):
            P.emit("scalar", s, sems)

        @block.vector
        def _(v):
            P.emit("vector", v, sems)

        @block.gpsimd
        def _(g):
            P.emit("gpsimd", g, sems)

        @block.sync
        def _(s):
            P.emit("sync", s, sems)

    return nc


def tile_fm(M, tw):
    T = M.shape[1]
    n = T // tw
    return np.ascontiguousarray(
        M.reshape(32, 128, n, tw).transpose(2, 1, 0, 3).reshape(n, 128, 32 * tw))


_shared_cache = {}


def make_shared(inputs):
    key = id(inputs.get("L0_ffn_W"))
    if _shared_cache.get("key") == key:
        return _shared_cache["val"]
    w_qkv = np.asarray(inputs["L0_attn_w_qkv"])
    Wq = np.asarray(inputs["L0_ffn_Wq"])
    Wk = np.asarray(inputs["L0_ffn_Wk"])
    A = np.asarray(inputs["L0_ffn_A"])
    Bm = np.asarray(inputs["L0_ffn_Bm"])
    BmT = Bm.transpose(0, 2, 1).reshape(128, 4096)
    wffn = tile_fm(np.asarray(inputs["L0_ffn_W"]).T, 128)
    wffn_t = np.concatenate(
        [wffn, BmT.reshape(128, 32, 128).transpose(1, 0, 2)], axis=2)
    val = {
        "wq_t": tile_fm(w_qkv[:, :4096], 128),
        "wc_t": tile_fm(np.asarray(inputs["L0_attn_w_c"]), 128),
        "wffn_t": np.ascontiguousarray(wffn_t),
        "wqka_t": tile_fm(np.concatenate(
            [Wq.T, Wk.T, A.reshape(128, 4096).T], axis=1), 640)[0],
        "wkv_t": tile_fm(w_qkv[:, 4096:4352], 256)[0],
        "bvec": np.ascontiguousarray(np.asarray(inputs["L0_ffn_b"]).reshape(1, 4096)),
        "cst": np.tile(np.array([[EPS, 1.0 / 4096, SQ32, 0.0]], np.float32), (128, 1)),
        "ident": np.eye(128, dtype=ml_dtypes.bfloat16),
        "onesb": np.ones((128, 128), dtype=ml_dtypes.bfloat16),
    }
    _shared_cache["key"] = key
    _shared_cache["val"] = val
    return val


def make_core_inputs(inputs, core):
    b, parity = core // 2, core % 2
    x = np.asarray(inputs["L0_attn_hidden_states"])
    xT_b = x[b].T
    own = np.concatenate(
        [xT_b[:, 1024 * i + 512 * parity: 1024 * i + 512 * parity + 512]
         for i in range(4)], axis=1)
    d = dict(make_shared(inputs))
    d.update({
        "xT_t": tile_fm(xT_b, 512),
        "xq_t": tile_fm(own, 512),
        "btab": host_bias_table(parity),
        "band": host_band(parity),
    })
    return d


def assemble_out(results):
    out = np.empty((B, S, E), np.float32)
    for core in range(8):
        b, parity = core // 2, core % 2
        o = np.asarray(results[core]["x2T"])
        for i in range(4):
            q0 = 1024 * i + 512 * parity
            out[b, q0:q0 + 512, :] = o[:, i * 512:(i + 1) * 512].T
    return out


def kernel(**inputs):
    nc = build_nc()
    in_maps = [make_core_inputs(inputs, c) for c in range(8)]
    res = run_bass_kernel_spmd(nc, in_maps, core_ids=list(range(8)))
    return assemble_out(res.results)
